# version 01 (final)
# baseline (speedup 1.0000x reference)
import numpy as np
import concourse.bass as bass
import concourse.mybir as mybir
from concourse.bass_utils import run_bass_kernel_spmd
from contextlib import ExitStack

F32 = mybir.dt.float32
BF16 = mybir.dt.bfloat16
AF = mybir.ActivationFunctionType
ALU = mybir.AluOpType
AX = mybir.AxisListType

D = 1024
KD = 8
H = 2048
NHT = 16
E = 8
TPC = 4096
NT = 32
NTH = 16
TC = 512
NTC = 4
TB = 2048
NCT = 8


class Sem:
    def __init__(self, stack, nc, name):
        self.h = stack.enter_context(nc.semaphore(name))
        self.n = 0

    def inc(self, instr, k=1):
        instr.then_inc(self.h, k)
        self.n += k
        return self.n


class PSem:
    def __init__(self, stack, nc, name, ways=2):
        self.s = [Sem(stack, nc, f"{name}_{i}") for i in range(ways)]

    def inc(self, instr, idx, k=16):
        i = idx % len(self.s)
        self.s[i].inc(instr, k)
        return (i, self.s[i].n)

    def wait(self, eng, val):
        i, n = val
        eng.wait_ge(self.s[i].h, n)


def _build():
    nc = bass.Bass("TRN2", target_bir_lowering=False)

    x0 = nc.dram_tensor("L0_attn_h", [TPC, D], F32, kind="ExternalInput").ap()
    w0a = nc.dram_tensor("L0_attn_W0", [D, 3 * D], F32, kind="ExternalInput").ap()
    w1a = nc.dram_tensor("L0_attn_W1", [D, 3 * D], F32, kind="ExternalInput").ap()
    wr = nc.dram_tensor("L0_ffn_Wr", [D, E], F32, kind="ExternalInput").ap()
    br = nc.dram_tensor("L0_ffn_br", [1, E], F32, kind="ExternalInput").ap()
    w1f = nc.dram_tensor("L0_ffn_W1", [E, D, H], F32, kind="ExternalInput").ap()
    b1f = nc.dram_tensor("L0_ffn_b1", [E, H], F32, kind="ExternalInput").ap()
    w2f = nc.dram_tensor("L0_ffn_W2", [E, H, D], F32, kind="ExternalInput").ap()
    b2f = nc.dram_tensor("L0_ffn_b2", [E, D], F32, kind="ExternalInput").ap()
    out = nc.dram_tensor("out", [TPC, D], F32, kind="ExternalOutput").ap()
    if _DEBUG_OUTS:
        x1s = nc.dram_tensor("x1s", [TPC, D], F32, kind="ExternalOutput").ap()
        x2s = nc.dram_tensor("x2s", [TPC, D], F32, kind="ExternalOutput").ap()
        ats = nc.dram_tensor("ats", [TPC, D], F32, kind="ExternalOutput").ap()
    else:
        x1s = nc.dram_tensor("x1s", [TPC, D], F32).ap()
        x2s = nc.dram_tensor("x2s", [TPC, D], F32).ap()
        ats = nc.dram_tensor("ats", [TPC, D], F32).ap()

    with ExitStack() as st:
        ent = st.enter_context
        ident_b = ent(nc.sbuf_tensor("ident_b", [128, 128], BF16))
        ident_f = ent(nc.sbuf_tensor("ident_f", [128, 128], F32))
        iot_p = ent(nc.sbuf_tensor("iot_p", [128, 1], F32))
        iot_f = ent(nc.sbuf_tensor("iot_f", [128, 128], F32))
        neghalf = ent(nc.sbuf_tensor("neghalf", [128, 1], F32))
        ones_b = ent(nc.sbuf_tensor("ones_b", [1, 128], BF16))
        brb = ent(nc.sbuf_tensor("brb", [1, E], BF16))
        wrb = ent(nc.sbuf_tensor("wrb", [128, KD * E], BF16))
        b1s = ent(nc.sbuf_tensor("b1s", [128, NHT * E], F32))
        b2b = ent(nc.sbuf_tensor("b2b", [E, D], BF16))
        b1raw = ent(nc.sbuf_tensor("b1raw", [E, H], F32))

        def S(name):
            return Sem(st, nc, name)

        def P(name):
            return PSem(st, nc, name)

        g0 = S("g0"); c0 = S("c0"); v0 = S("v0"); t0 = S("t0"); e0 = S("e0")
        with nc.Block() as block:
            @block.gpsimd
            def _(g):
                c0.inc(g.iota(iot_p[:, :], [[0, 1]], channel_multiplier=1,
                              allow_small_or_imprecise_dtypes=True))
                c0.inc(g.iota(iot_f[:, :], [[1, 128]], channel_multiplier=0,
                              allow_small_or_imprecise_dtypes=True))
                c0.inc(g.memset(neghalf[:, :], -0.5))
                c0.inc(g.memset(ones_b[:, :], 1.0))
                g0.inc(g.dma_start(out=brb[:, :], in_=br), 16)
                g0.inc(g.dma_start(out=wrb[:, :],
                                   in_=wr.rearrange("(k p) e -> p k e", p=128)), 16)
                g0.inc(g.dma_start(out=b2b[:, :], in_=b2f), 16)
                g0.inc(g.dma_start(out=b1raw[:, :], in_=b1f), 16)

            @block.vector
            def _(v):
                v.wait_ge(c0.h, 2)
                v0.inc(v.tensor_scalar(out=ident_b[:, :], in0=iot_f[:, :],
                                       scalar1=iot_p[:, :], scalar2=None,
                                       op0=ALU.is_equal))
                v0.inc(v.tensor_scalar(out=ident_f[:, :], in0=iot_f[:, :],
                                       scalar1=iot_p[:, :], scalar2=None,
                                       op0=ALU.is_equal))

        with nc.psum_tensor("p0t", [128, E], F32) as p0t, nc.Block() as block:
            for k in range(NHT):
                @block.tensor
                def _(t, k=k):
                    if k == 0:
                        t.wait_ge(v0.h, 2)
                        t.wait_ge(g0.h, 64)
                    else:
                        t.wait_ge(e0.h, k)
                    t0.inc(t.transpose(p0t[:, :], b1raw[0:E, k * 128:(k + 1) * 128],
                                       ident_f[0:E, 0:E]))

                @block.vector
                def _(v, k=k):
                    v.wait_ge(t0.h, k + 1)
                    e0.inc(v.tensor_copy(out=b1s[:, k * E:(k + 1) * E], in_=p0t[:, :]))

        def emit_ln_tile(block, t, xin, sqs, cols, lnc_p, lnT, ptrp, sems, rec,
                        pre_scalar_wait=None):
            qx, va, sL, tT, vE = sems["qx"], sems["va"], sems["sL"], sems["tT"], sems["vE"]
            p = t % 2
            c = p * 12
            tr = t % NTH

            @block.vector
            def _(v):
                qx.wait(v, rec[("qx", t)])
                pv = rec.get(("pre_vec", t))
                if pv:
                    v.wait_ge(va.h, pv)
                va.inc(v.tensor_reduce(out=cols[:, c:c + 1], in_=xin[:, :],
                                       axis=AX.X, op=ALU.add))
                v.wait_ge(va.h, va.n)
                va.inc(v.tensor_scalar(out=cols[:, c + 1:c + 2], in0=cols[:, c:c + 1],
                                       scalar1=1.0 / D, scalar2=None, op0=ALU.mult))
            rec[("va_m", t)] = va.n

            @block.scalar
            def _(s):
                qx.wait(s, rec[("qx", t)])
                if pre_scalar_wait is not None:
                    pre_scalar_wait(s)
                sL.inc(s.activation(out=sqs[:, :], in_=xin[:, :], func=AF.Square,
                                    scale=1.0, accum_out=cols[:, c + 2:c + 3]))
            rec[("sq", t)] = sL.n

            @block.vector
            def _(v):
                v.wait_ge(sL.h, rec[("sq", t)])
                v.wait_ge(va.h, rec[("va_m", t)])
                va.inc(v.tensor_scalar(out=cols[:, c + 3:c + 4], in0=cols[:, c + 2:c + 3],
                                       scalar1=1.0 / D, scalar2=None, op0=ALU.mult))
                va.inc(v.tensor_tensor(out=cols[:, c + 4:c + 5], in0=cols[:, c + 1:c + 2],
                                       in1=cols[:, c + 1:c + 2], op=ALU.mult))
                v.wait_ge(va.h, va.n)
                va.inc(v.tensor_tensor(out=cols[:, c + 5:c + 6], in0=cols[:, c + 3:c + 4],
                                       in1=cols[:, c + 4:c + 5], op=ALU.subtract))
                v.wait_ge(va.h, va.n)
                va.inc(v.tensor_scalar(out=cols[:, c + 6:c + 7], in0=cols[:, c + 5:c + 6],
                                       scalar1=1e-5, scalar2=None, op0=ALU.add))
            rec[("va_u", t)] = va.n

            @block.scalar
            def _(s):
                s.wait_ge(va.h, rec[("va_u", t)])
                sL.inc(s.activation(out=cols[:, c + 7:c + 8], in_=cols[:, c + 6:c + 7],
                                    func=AF.Ln, scale=1.0))
            rec[("lnu", t)] = sL.n

            @block.scalar
            def _(s):
                s.wait_ge(sL.h, rec[("lnu", t)])
                sL.inc(s.activation(out=cols[:, c + 8:c + 9], in_=cols[:, c + 7:c + 8],
                                    func=AF.Exp, scale=neghalf[:, :]))
            rec[("r", t)] = sL.n

            @block.vector
            def _(v):
                v.wait_ge(sL.h, rec[("r", t)])
                v.wait_ge(va.h, rec[("va_m", t)])
                va.inc(v.scalar_tensor_tensor(out=cols[:, c + 9:c + 10],
                                              in0=cols[:, c + 1:c + 2], scalar=-1.0,
                                              in1=cols[:, c + 8:c + 9],
                                              op0=ALU.mult, op1=ALU.mult))
            rec[("nmr", t)] = va.n

            @block.scalar
            def _(s):
                s.wait_ge(va.h, rec[("nmr", t)])
                u = rec.get(("lncuse", p))
                if u:
                    s.wait_ge(tT.h, u)
                sL.inc(s.activation(out=lnc_p[:, :], in_=xin[:, :], func=AF.Identity,
                                    bias=cols[:, c + 9:c + 10],
                                    scale=cols[:, c + 8:c + 9]))
            rec[("ln", t)] = sL.n

            @block.tensor
            def _(t_):
                t_.wait_ge(sL.h, rec[("ln", t)])
                u = rec.get(("ptruse", p))
                if u:
                    t_.wait_ge(vE.h, u)
                for j in range(KD):
                    mm = t_.transpose(ptrp[:, j * 128:(j + 1) * 128],
                                      lnc_p[:, j * 128:(j + 1) * 128], ident_b[:, :])
                tT.inc(mm)
            rec[("tr", t)] = tT.n
            rec[("lncuse", p)] = tT.n

            @block.vector
            def _(v):
                v.wait_ge(tT.h, rec[("tr", t)])
                w = rec.get(("lnT_war",))
                if w:
                    eng_sem, val = w
                    v.wait_ge(eng_sem.h, val)
                dst = lnT.ap().rearrange("p (k tt cc) -> p k tt cc",
                                         k=KD, tt=NTH)[:, :, tr, :]
                vE.inc(v.tensor_copy(out=dst,
                                     in_=ptrp.ap().rearrange("p (k cc) -> p k cc", k=KD)))
            rec[("lnTev", t)] = vE.n
            rec[("ptruse", p)] = vE.n

        def emit_ffn(ci):
            sfx = f"f{ci}"
            with ExitStack() as fs, nc.Block() as block:
                fent = fs.enter_context
                xa = [fent(nc.sbuf_tensor(f"xa{i}{sfx}", [128, D], F32)) for i in range(2)]
                xb = [fent(nc.sbuf_tensor(f"xb{i}{sfx}", [128, D], F32)) for i in range(2)] if ci else None
                sqs = fent(nc.sbuf_tensor(f"sqs{sfx}", [128, D], F32))
                lnc = [fent(nc.sbuf_tensor(f"lnc{i}{sfx}", [128, D], BF16)) for i in range(2)]
                cols = fent(nc.sbuf_tensor(f"cols{sfx}", [128, 24], F32))
                msk = fent(nc.sbuf_tensor(f"msk{sfx}", [128, E], F32))
                qsc = fent(nc.sbuf_tensor(f"qsc{sfx}", [128, E], F32))
                lnT = fent(nc.sbuf_tensor(f"lnT{sfx}", [128, KD * 2048], BF16))
                s_all = fent(nc.sbuf_tensor(f"sall{sfx}", [128, NT * E], F32))
                sallT = fent(nc.sbuf_tensor(f"sallT{sfx}", [E, NT * 128], BF16))
                w1s = fent(nc.sbuf_tensor(f"w1s{sfx}", [128, KD * H], BF16))
                w2s = fent(nc.sbuf_tensor(f"w2s{sfx}", [128, NHT * D], BF16))
                heb = fent(nc.sbuf_tensor(f"heb{sfx}", [128, NHT * TC], BF16))
                acc = fent(nc.sbuf_tensor(f"acc{sfx}", [128, NTH * D], BF16))
                xc = [fent(nc.sbuf_tensor(f"xc{i}{sfx}", [128, D], F32)) for i in range(2)]
                outb = [fent(nc.sbuf_tensor(f"outb{i}{sfx}", [128, D], F32)) for i in range(2)]
                ptr = [fent(nc.psum_tensor(f"ptr{i}{sfx}", [128, KD * 128], BF16)) for i in range(2)]
                pAB = [fent(nc.psum_tensor(f"pAB{i}{sfx}", [128, 512], F32)) for i in range(2)]
                pm1 = [fent(nc.psum_tensor(f"pm1{i}{sfx}", [128, TC], F32)) for i in range(2)]
                pm2 = [fent(nc.psum_tensor(f"pm2{i}{sfx}", [128, 512], F32)) for i in range(2)]

                qx = P("qx" + sfx)
                qc = P("qc" + sfx)
                qw2 = P("qw2" + sfx) if ci else None
                qoc = P("qoc" + sfx)
                gw1 = S("gw1" + sfx); gw2 = S("gw2" + sfx)
                gm = S("gm" + sfx)
                va = S("va" + sfx); vE = S("vE" + sfx); vB = S("vB" + sfx); vC = S("vC" + sfx)
                sL = S("sL" + sfx); sQ = S("sQ" + sfx); sE = S("sE" + sfx)
                tT = S("tT" + sfx); tL = S("tL" + sfx)
                tM1 = S("tM1" + sfx); tM2 = S("tM2" + sfx); tB2 = S("tB2" + sfx)
                sems = dict(qx=qx, va=va, sL=sL, tT=tT, vE=vE)
                rec = {}

                def passA(half):
                    for tr in range(NTH):
                        t = half * NTH + tr
                        p = t % 2

                        @block.sync
                        def _(sy):
                            u = rec.get(("ln", t - 2))
                            if u:
                                sy.wait_ge(sL.h, u)
                            u = rec.get(("xause", p))
                            if u:
                                qw2.wait(sy, u)
                            src = x0 if ci == 0 else x1s
                            qx.inc(sy.dma_start(out=xa[p][:, :],
                                                in_=src[t * 128:(t + 1) * 128, :]), t)
                            if ci:
                                rec[("qx", t)] = qx.inc(
                                    sy.dma_start(out=xb[p][:, :],
                                                 in_=ats[t * 128:(t + 1) * 128, :]), t)
                            else:
                                rec[("qx", t)] = (t % 2, qx.s[t % 2].n)

                        if ci:
                            @block.vector
                            def _(v):
                                qx.wait(v, rec[("qx", t)])
                                va.inc(v.tensor_tensor(out=xa[p][:, :], in0=xa[p][:, :],
                                                       in1=xb[p][:, :], op=ALU.add))
                            rec[("xv", t)] = va.n
                            rec[("pre_vec", t)] = va.n

                            @block.sync
                            def _(sy):
                                sy.wait_ge(va.h, rec[("xv", t)])
                                rec[("x2w", t)] = qw2.inc(
                                    sy.dma_start(out=x2s[t * 128:(t + 1) * 128, :],
                                                 in_=xa[p][:, :]), t)
                            rec[("xause", p)] = rec[("x2w", t)]

                        xin = xa[p]
                        pre = (lambda s, tt=t: s.wait_ge(va.h, rec[("xv", tt)])) if ci else None
                        emit_ln_tile(block, t, xin, sqs, cols, lnc[p], lnT, ptr[p],
                                     sems, rec, pre_scalar_wait=pre)

                        @block.tensor
                        def _(t_):
                            t_.wait_ge(vE.h, rec[("lnTev", t)])
                            u = rec.get(("pABuse_v", p))
                            if u:
                                t_.wait_ge(va.h, u)
                            u = rec.get(("pABuse_s", p))
                            if u:
                                t_.wait_ge(sQ.h, u)
                            u = rec.get(("pABuse_e", p))
                            if u:
                                t_.wait_ge(vE.h, u)
                            u = rec.get(("pABuse_c", p))
                            if u:
                                t_.wait_ge(vC.h, u)
                            for k in range(KD):
                                t_.matmul(pAB[p][:, 0:E],
                                          lnT[:, k * 2048 + tr * 128:k * 2048 + (tr + 1) * 128],
                                          wrb[:, k * E:(k + 1) * E],
                                          start=(k == 0), stop=False)
                            tL.inc(t_.matmul(pAB[p][:, 0:E], ones_b[:, :], brb[:, :],
                                             start=False, stop=True))
                        rec[("lg", t)] = tL.n

                        @block.vector
                        def _(v):
                            c = p * 12
                            v.wait_ge(tL.h, rec[("lg", t)])
                            va.inc(v.tensor_reduce(out=cols[:, c + 10:c + 11],
                                                   in_=pAB[p][:, 0:E], axis=AX.X,
                                                   op=ALU.max, negate=True))
                        rec[("nmx", t)] = va.n

                        @block.scalar
                        def _(s):
                            c = p * 12
                            s.wait_ge(va.h, rec[("nmx", t)])
                            sQ.inc(s.activation(out=qsc[:, :], in_=pAB[p][:, 0:E],
                                                func=AF.Exp, bias=cols[:, c + 10:c + 11],
                                                scale=1.0,
                                                accum_out=cols[:, c + 11:c + 12]))
                        rec[("q", t)] = sQ.n
                        rec[("pABuse_s", p)] = sQ.n

                        @block.vector
                        def _(v):
                            c = p * 12
                            v.wait_ge(sQ.h, rec[("q", t)])
                            va.inc(v.reciprocal(out=cols[:, c + 0:c + 1],
                                                in_=cols[:, c + 11:c + 12]))
                        rec[("recip", t)] = va.n

                        @block.vector
                        def _(v):
                            c = p * 12
                            v.wait_ge(va.h, rec[("recip", t)])
                            va.inc(v.tensor_scalar(out=msk[:, :], in0=pAB[p][:, 0:E],
                                                   scalar1=cols[:, c + 10:c + 11],
                                                   scalar2=0.0, op0=ALU.add, op1=ALU.is_ge))
                        rec[("mask", t)] = va.n
                        rec[("pABuse_v", p)] = va.n

                        @block.vector
                        def _(v):
                            c = p * 12
                            v.wait_ge(va.h, rec[("mask", t)])
                            va.inc(v.tensor_scalar(out=s_all[:, t * E:(t + 1) * E],
                                                   in0=msk[:, :],
                                                   scalar1=cols[:, c + 0:c + 1],
                                                   scalar2=0.5, op0=ALU.mult, op1=ALU.mult))
                        rec[("sall", t)] = va.n

                        @block.tensor
                        def _(t_):
                            t_.wait_ge(va.h, rec[("sall", t)])
                            tT.inc(t_.transpose(pAB[p][0:E, 16:16 + 128],
                                                s_all[:, t * E:(t + 1) * E],
                                                ident_f[:, :]))
                        rec[("strT", t)] = tT.n

                        @block.vector
                        def _(v):
                            v.wait_ge(tT.h, rec[("strT", t)])
                            vE.inc(v.tensor_copy(out=sallT[0:E, t * 128:(t + 1) * 128],
                                                 in_=pAB[p][0:E, 16:16 + 128]))
                        rec[("strE", t)] = vE.n
                        rec[("pABuse_e", p)] = vE.n

                def passB(half):
                    @block.gpsimd
                    def _(g):
                        u = rec.get(("accuse",))
                        if u:
                            g.wait_ge(vC.h, u)
                            g.wait_ge(vB.h, rec[("accwrit",)])
                        gm.inc(g.memset(acc.ap(), 0.0))
                    rec[("ms", half)] = gm.n

                    for e in range(E):
                        eg = half * E + e

                        @block.gpsimd
                        def _(g):
                            u = rec.get(("mm1_last",))
                            if u:
                                g.wait_ge(tM1.h, u)
                            gw1.inc(g.dma_start(
                                out=w1s.ap(),
                                in_=w1f[e].rearrange("(k p) h -> p k h", p=128)), 16)
                        rec[("w1", eg)] = gw1.n

                        @block.gpsimd
                        def _(g):
                            u = rec.get(("mm2_last",))
                            if u:
                                g.wait_ge(tM2.h, u)
                            gw2.inc(g.dma_start(
                                out=w2s.ap(),
                                in_=w2f[e].rearrange("(k p) d -> p k d", p=128)), 16)
                        rec[("w2", eg)] = gw2.n

                        for tc in range(NTC):
                            for ht in range(NHT):
                                pp = ht % 2

                                @block.tensor
                                def _(t_):
                                    if ht == 0 and tc == 0:
                                        t_.wait_ge(gw1.h, rec[("w1", eg)])
                                        if e == 0:
                                            t_.wait_ge(vE.h, rec[("lnTev", half * NTH + NTH - 1)])
                                    u = rec.get(("pm1use", pp))
                                    if u:
                                        t_.wait_ge(sE.h, u)
                                    for k in range(KD):
                                        mm = t_.matmul(
                                            pm1[pp][:, :],
                                            w1s[:, k * H + ht * 128:k * H + (ht + 1) * 128],
                                            lnT[:, k * 2048 + tc * TC:k * 2048 + (tc + 1) * TC],
                                            start=(k == 0), stop=(k == KD - 1))
                                    tM1.inc(mm)
                                rec[("mm1", eg, tc, ht)] = tM1.n

                                @block.scalar
                                def _(s):
                                    s.wait_ge(tM1.h, rec[("mm1", eg, tc, ht)])
                                    u = rec.get(("hebuse",))
                                    if u:
                                        s.wait_ge(tM2.h, u)
                                    sE.inc(s.activation(
                                        out=heb[:, ht * TC:(ht + 1) * TC],
                                        in_=pm1[pp][:, :], func=AF.Relu,
                                        bias=b1s[:, ht * E + e:ht * E + e + 1],
                                        scale=1.0))
                                rec[("ev1", eg, tc, ht)] = sE.n
                                rec[("pm1use", pp)] = sE.n

                            for m in range(4):
                                gtile = tc * 4 + m
                                tglob = half * NTH + gtile
                                for dch in range(2):
                                    gi = (m * 2 + dch) % 2

                                    @block.tensor
                                    def _(t_):
                                        if m == 0 and dch == 0:
                                            t_.wait_ge(sE.h, rec[("ev1", eg, tc, NHT - 1)])
                                            if tc == 0:
                                                t_.wait_ge(gw2.h, rec[("w2", eg)])
                                        u = rec.get(("pm2use", gi))
                                        if u:
                                            t_.wait_ge(vB.h, u)
                                        for kh in range(NHT):
                                            mm = t_.matmul(
                                                pm2[gi][:, :],
                                                heb[:, kh * TC + m * 128:kh * TC + (m + 1) * 128],
                                                w2s[:, kh * D + dch * 512:kh * D + (dch + 1) * 512],
                                                start=(kh == 0), stop=(kh == NHT - 1))
                                        tM2.inc(mm)
                                    rec[("mm2", eg, tc, m, dch)] = tM2.n

                                    @block.vector
                                    def _(v):
                                        v.wait_ge(tM2.h, rec[("mm2", eg, tc, m, dch)])
                                        if e == 0:
                                            v.wait_ge(gm.h, rec[("ms", half)])
                                        u = rec.get(("accslot", gtile, dch))
                                        if u:
                                            v.wait_ge(vB.h, u)
                                        sl = acc[:, gtile * D + dch * 512:
                                                 gtile * D + (dch + 1) * 512]
                                        vB.inc(v.scalar_tensor_tensor(
                                            out=sl, in0=pm2[gi][:, :],
                                            scalar=s_all[:, tglob * E + e:tglob * E + e + 1],
                                            in1=sl, op0=ALU.mult, op1=ALU.add))
                                    rec[("pm2use", gi)] = vB.n
                                    rec[("accslot", gtile, dch)] = vB.n
                            rec[("hebuse",)] = tM2.n
                        rec[("mm1_last",)] = tM1.n
                        rec[("mm2_last",)] = tM2.n
                    rec[("accwrit",)] = vB.n
                    if half == 0:
                        rec[("lnT_war",)] = (tM1, tM1.n)

                def passC(half):
                    src = x0 if ci == 0 else x2s
                    dst = x1s if ci == 0 else out
                    for tr in range(NTH):
                        t = half * NTH + tr
                        p = t % 2

                        @block.sync
                        def _(sy):
                            if ci:
                                qw2.wait(sy, rec[("x2w", t)])
                            u = rec.get(("xcuse", p))
                            if u:
                                sy.wait_ge(vC.h, u)
                            rec[("cqx", t)] = qc.inc(
                                sy.dma_start(out=xc[p][:, :],
                                             in_=src[t * 128:(t + 1) * 128, :]), t)

                        for dch in range(2):
                            @block.tensor
                            def _(t_):
                                if tr == 0 and dch == 0:
                                    t_.wait_ge(tM2.h, rec[("mm2_last",)])
                                    t_.wait_ge(vE.h, rec[("strE", half * NTH + NTH - 1)])
                                u = rec.get(("pABuse_c2", dch))
                                if u:
                                    t_.wait_ge(vC.h, u)
                                tB2.inc(t_.matmul(pAB[dch][:, 0:512],
                                                  sallT[0:E, t * 128:(t + 1) * 128],
                                                  b2b[0:E, dch * 512:(dch + 1) * 512],
                                                  start=True, stop=True))
                            rec[("b2", t, dch)] = tB2.n

                            @block.vector
                            def _(v):
                                v.wait_ge(tB2.h, rec[("b2", t, dch)])
                                u = rec.get(("outbuse", p))
                                if u:
                                    qoc.wait(v, u)
                                v.wait_ge(vB.h, rec[("accslot", tr, dch)])
                                sl = acc[:, tr * D + dch * 512:tr * D + (dch + 1) * 512]
                                vC.inc(v.scalar_tensor_tensor(
                                    out=outb[p][:, dch * 512:(dch + 1) * 512],
                                    in0=pAB[dch][:, 0:512], scalar=1.0, in1=sl,
                                    op0=ALU.mult, op1=ALU.add))
                            rec[("b2e", t, dch)] = vC.n
                            rec[("pABuse_c2", dch)] = vC.n
                            rec[("pABuse_c", dch)] = vC.n

                        @block.vector
                        def _(v):
                            qc.wait(v, rec[("cqx", t)])
                            v.wait_ge(vC.h, rec[("b2e", t, 1)])
                            vC.inc(v.tensor_tensor(out=outb[p][:, :], in0=outb[p][:, :],
                                                   in1=xc[p][:, :], op=ALU.add))
                        rec[("cadd", t)] = vC.n
                        rec[("xcuse", p)] = vC.n

                        @block.sync
                        def _(sy):
                            sy.wait_ge(vC.h, rec[("cadd", t)])
                            rec[("cw", t)] = qoc.inc(
                                sy.dma_start(out=dst[t * 128:(t + 1) * 128, :],
                                             in_=outb[p][:, :]), t)
                        rec[("outbuse", p)] = rec[("cw", t)]
                    rec[("accuse",)] = vC.n

                for half in (0, 1):
                    passA(half)
                    passB(half)
                    passC(half)

                @block.sync
                def _(sy):
                    for ps in qoc.s:
                        sy.wait_ge(ps.h, ps.n)
                    if ci:
                        for ps in qw2.s:
                            sy.wait_ge(ps.h, ps.n)

        def emit_attn():
            sfx = "at"
            with ExitStack() as fs, nc.Block() as block:
                fent = fs.enter_context
                xa = [fent(nc.sbuf_tensor(f"xa{i}{sfx}", [128, D], F32)) for i in range(2)]
                sqa = fent(nc.sbuf_tensor(f"sqa{sfx}", [128, D], F32))
                lnc = [fent(nc.sbuf_tensor(f"lnc{i}{sfx}", [128, D], BF16)) for i in range(2)]
                cols = fent(nc.sbuf_tensor(f"cols{sfx}", [128, 24], F32))
                lnTa = fent(nc.sbuf_tensor(f"lnTa{sfx}", [128, KD * TB], BF16))
                h1T = fent(nc.sbuf_tensor(f"h1T{sfx}", [128, KD * TB], BF16))
                wsl = [[fent(nc.sbuf_tensor(f"wsl{i}{j}{sfx}", [128, KD * 128], BF16))
                        for j in range(3)] for i in range(2)]
                cbu = [fent(nc.sbuf_tensor(f"cbu{i}{sfx}", [128, TB], F32)) for i in range(2)]
                vbu = [fent(nc.sbuf_tensor(f"vbu{i}{sfx}", [128, TB], F32)) for i in range(2)]
                zsb = [fent(nc.sbuf_tensor(f"zsb{i}{sfx}", [128, 512], F32)) for i in range(2)]
                obu = fent(nc.sbuf_tensor(f"obu{sfx}", [128, TB], F32))
                gpb = [fent(nc.sbuf_tensor(f"gpb{i}{sfx}", [128, TB], BF16)) for i in range(2)]
                hmn = [fent(nc.sbuf_tensor(f"hmn{i}{sfx}", [128, 512], F32)) for i in range(2)]
                gtb = [fent(nc.sbuf_tensor(f"gtb{i}{sfx}", [128, 512], F32)) for i in range(2)]
                dsc = fent(nc.sbuf_tensor(f"dsc{sfx}", [128, TB], F32))
                t2s = fent(nc.sbuf_tensor(f"t2s{sfx}", [128, TB], F32))
                hnf = fent(nc.sbuf_tensor(f"hnf{sfx}", [128, TB], F32))
                stg = fent(nc.sbuf_tensor(f"stg{sfx}", [128, TB], F32))
                ptr = [fent(nc.psum_tensor(f"ptr{i}{sfx}", [128, KD * 128], BF16)) for i in range(2)]
                pz = [fent(nc.psum_tensor(f"pz{i}{sfx}", [128, 512], F32)) for i in range(4)]
                pt = [fent(nc.psum_tensor(f"pt{i}{sfx}", [128, 512], F32)) for i in range(2)]

                qx = P("qx" + sfx)
                qo = S("qo" + sfx)
                gw = P("gw" + sfx)
                va = S("va" + sfx); vE = S("vE" + sfx); vM = S("vM" + sfx)
                sL = S("sL" + sfx); sV = S("sV" + sfx)
                tT = S("tT" + sfx); tZ = S("tZ" + sfx)
                sems = dict(qx=qx, va=va, sL=sL, tT=tT, vE=vE)
                rec = {}

                for b in range(2):
                    for tr in range(NTH):
                        t = b * NTH + tr
                        p = t % 2

                        @block.sync
                        def _(sy):
                            u = rec.get(("ln", t - 2))
                            if u:
                                sy.wait_ge(sL.h, u)
                            rec[("qx", t)] = qx.inc(
                                sy.dma_start(out=xa[p][:, :],
                                             in_=x1s[t * 128:(t + 1) * 128, :]), t)

                        emit_ln_tile(block, t, xa[p], sqa, cols, lnc[p], lnTa, ptr[p],
                                     sems, rec)

                    for it in range(2):
                        wsrc = w0a if it == 0 else w1a
                        hprev = lnTa if it == 0 else h1T
                        for ct in range(NCT):
                            wp = ct % 2
                            cp = ct % 2

                            @block.gpsimd
                            def _(g):
                                u = rec.get(("wsluse", wp))
                                if u:
                                    g.wait_ge(tZ.h, u)
                                for j, base in enumerate((D, 0, 2 * D)):
                                    wv = gw.inc(g.dma_start(
                                        out=wsl[wp][j].ap(),
                                        in_=wsrc[:, base + ct * 128:base + (ct + 1) * 128]
                                        .rearrange("(k p) cc -> p k cc", p=128)), ct)
                                rec[("wsl", b, it, ct)] = wv

                            for ch in range(4):
                                sl = slice(ch * 512, (ch + 1) * 512)
                                chp = ch % 2
                                for j in range(3):
                                    pzi = (ch * 3 + j) % 4

                                    @block.tensor
                                    def _(t_):
                                        if j == 0 and ch == 0:
                                            gw.wait(t_, rec[("wsl", b, it, ct)])
                                            if it == 0 and ct == 0:
                                                t_.wait_ge(vE.h, rec[("lnTev", b * NTH + NTH - 1)])
                                            if it == 1 and ct == 0:
                                                t_.wait_ge(vM.h, rec[("mixall", b, 0)])
                                        u = rec.get(("pzuse", pzi))
                                        if u:
                                            eng, val = u
                                            if eng == "s":
                                                t_.wait_ge(sV.h, val)
                                            else:
                                                t_.wait_ge(va.h, val)
                                        for k in range(KD):
                                            mm = t_.matmul(
                                                pz[pzi][:, :],
                                                wsl[wp][j][:, k * 128:(k + 1) * 128],
                                                hprev[:, k * TB + ch * 512:k * TB + (ch + 1) * 512],
                                                start=(k == 0), stop=(k == KD - 1))
                                        tZ.inc(mm)
                                    rec[("z", b, it, ct, j, ch)] = tZ.n

                                    if j == 0:
                                        @block.scalar
                                        def _(s):
                                            s.wait_ge(tZ.h, rec[("z", b, it, ct, j, ch)])
                                            u = rec.get(("cbuuse", cp))
                                            if u:
                                                s.wait_ge(vM.h, u)
                                            u = rec.get(("zsbuse", chp))
                                            if u:
                                                s.wait_ge(va.h, u)
                                            sV.inc(s.activation(out=cbu[cp][:, sl],
                                                                in_=pz[pzi][:, :],
                                                                func=AF.Sigmoid, scale=-1.0))
                                            sV.inc(s.activation(out=zsb[chp][:, :],
                                                                in_=pz[pzi][:, :],
                                                                func=AF.Sigmoid, scale=1.0))
                                        rec[("pzuse", pzi)] = ("s", sV.n)
                                        rec[("gate", b, it, ct, ch)] = sV.n
                                    elif j == 1:
                                        @block.vector
                                        def _(v):
                                            v.wait_ge(tZ.h, rec[("z", b, it, ct, j, ch)])
                                            u = rec.get(("hmnuse", chp))
                                            if u:
                                                v.wait_ge(sV.h, u)
                                            va.inc(v.tensor_scalar(out=hmn[chp][:, :],
                                                                   in0=pz[pzi][:, :],
                                                                   scalar1=0.0, scalar2=None,
                                                                   op0=ALU.min))
                                        rec[("hmn", b, it, ct, ch)] = va.n

                                        @block.scalar
                                        def _(s):
                                            s.wait_ge(va.h, rec[("hmn", b, it, ct, ch)])
                                            u = rec.get(("gtbuse", chp))
                                            if u:
                                                s.wait_ge(va.h, u)
                                            sV.inc(s.activation(out=gtb[chp][:, :],
                                                                in_=hmn[chp][:, :],
                                                                func=AF.Sigmoid, scale=1.0))
                                        rec[("sg", b, it, ct, ch)] = sV.n
                                        rec[("hmnuse", chp)] = sV.n

                                        @block.vector
                                        def _(v):
                                            v.wait_ge(sV.h, rec[("sg", b, it, ct, ch)])
                                            va.inc(v.scalar_tensor_tensor(
                                                out=gtb[chp][:, :], in0=pz[pzi][:, :],
                                                scalar=0.0, in1=gtb[chp][:, :],
                                                op0=ALU.max, op1=ALU.add))
                                            v.wait_ge(sV.h, rec[("gate", b, it, ct, ch)])
                                            v.wait_ge(va.h, va.n)
                                            va.inc(v.tensor_tensor(out=vbu[cp][:, sl],
                                                                   in0=gtb[chp][:, :],
                                                                   in1=zsb[chp][:, :],
                                                                   op=ALU.mult))
                                        rec[("pzuse", pzi)] = ("v", va.n)
                                        rec[("vv", b, it, ct, ch)] = va.n
                                        rec[("gtbuse", chp)] = va.n
                                        rec[("zsbuse", chp)] = va.n
                                    else:
                                        @block.scalar
                                        def _(s):
                                            s.wait_ge(tZ.h, rec[("z", b, it, ct, j, ch)])
                                            u = rec.get(("gpbuse", cp))
                                            if u:
                                                s.wait_ge(vM.h, u)
                                            sV.inc(s.activation(out=gpb[cp][:, sl],
                                                                in_=pz[pzi][:, :],
                                                                func=AF.Sigmoid, scale=1.0))
                                        rec[("pzuse", pzi)] = ("s", sV.n)
                                        rec[("gp", b, it, ct, ch)] = sV.n
                            rec[("wsluse", wp)] = tZ.n

                            @block.vector
                            def _(v):
                                v.wait_ge(sV.h, rec[("gate", b, it, ct, 3)])
                                v.wait_ge(va.h, rec[("vv", b, it, ct, 3)])
                                u = rec.get(("obuuse",))
                                if u:
                                    v.wait_ge(va.h, u)
                                vM.inc(v.tensor_tensor_scan(
                                    out=obu[:, :], data0=cbu[cp][:, :],
                                    data1=vbu[cp][:, :], initial=0.0,
                                    op0=ALU.mult, op1=ALU.add))
                            rec[("scan", b, it, ct)] = vM.n
                            rec[("cbuuse", cp)] = vM.n

                            hp = hprev[:, ct * TB:(ct + 1) * TB]

                            @block.vector
                            def _(v):
                                v.wait_ge(sV.h, rec[("gp", b, it, ct, 3)])
                                v.wait_ge(vM.h, rec[("scan", b, it, ct)])
                                if it == 1:
                                    u = rec.get(("hnfuse",))
                                    if u:
                                        v.wait_ge(tT.h, u)
                                    v.wait_ge(vM.h, rec[("mix", b, 0, ct)])
                                else:
                                    v.wait_ge(vE.h, rec[("lnTev", b * NTH + NTH - 1)])
                                    if b == 1:
                                        v.wait_ge(tZ.h, rec[("h1Tuse",)])
                                u = rec.get(("mixprev",))
                                if u:
                                    v.wait_ge(vM.h, u)
                                va.inc(v.tensor_tensor(out=dsc[:, :], in0=obu[:, :],
                                                       in1=hp, op=ALU.subtract))
                                rec[("obuuse",)] = va.n
                                v.wait_ge(va.h, va.n)
                                va.inc(v.tensor_tensor(out=t2s[:, :], in0=dsc[:, :],
                                                       in1=gpb[cp][:, :], op=ALU.mult))
                                v.wait_ge(va.h, va.n)
                                dst_h = h1T[:, ct * TB:(ct + 1) * TB] if it == 0 else hnf[:, :]
                                vM.inc(v.tensor_tensor(out=dst_h, in0=hp, in1=t2s[:, :],
                                                       op=ALU.add))
                            rec[("mix", b, it, ct)] = vM.n
                            rec[("mixprev",)] = vM.n
                            rec[("gpbuse", cp)] = vM.n
                            if ct == NCT - 1:
                                rec[("mixall", b, it)] = vM.n

                            if it == 1:
                                for grp in range(4):
                                    pti = grp % 2

                                    @block.tensor
                                    def _(t_):
                                        t_.wait_ge(vM.h, rec[("mix", b, it, ct)])
                                        u = rec.get(("ptuse", pti))
                                        if u:
                                            t_.wait_ge(vE.h, u)
                                        for q in range(4):
                                            mm = t_.transpose(
                                                pt[pti][:, q * 128:(q + 1) * 128],
                                                hnf[:, (grp * 4 + q) * 128:
                                                    (grp * 4 + q + 1) * 128],
                                                ident_f[:, :])
                                        tT.inc(mm)
                                    rec[("htr", b, ct, grp)] = tT.n

                                    @block.vector
                                    def _(v):
                                        v.wait_ge(tT.h, rec[("htr", b, ct, grp)])
                                        u = rec.get(("stguse",))
                                        if u:
                                            v.wait_ge(qo.h, u)
                                        vE.inc(v.tensor_copy(
                                            out=stg[:, grp * 512:(grp + 1) * 512],
                                            in_=pt[pti][:, :]))
                                    rec[("ptuse", pti)] = vE.n
                                    rec[("stge", b, ct, grp)] = vE.n
                                rec[("hnfuse",)] = tT.n

                                @block.sync
                                def _(sy):
                                    sy.wait_ge(vE.h, rec[("stge", b, ct, 3)])
                                    qo.inc(sy.dma_start(
                                        out=ats[b * TB:(b + 1) * TB,
                                                ct * 128:(ct + 1) * 128]
                                        .rearrange("(tt p) cc -> p tt cc", p=128),
                                        in_=stg.ap().rearrange("p (tt cc) -> p tt cc",
                                                               cc=128)), 16)
                                rec[("stgw", b, ct)] = qo.n
                                rec[("stguse",)] = qo.n

                    rec[("lnT_war",)] = (tZ, tZ.n)
                    rec[("h1Tuse",)] = tZ.n

                @block.sync
                def _(sy):
                    sy.wait_ge(qo.h, qo.n)

        emit_ffn(0)
        emit_attn()
        emit_ffn(1)

    return nc


_NC_CACHE = {}
_DEBUG_OUTS = False
_LAST_RESULTS = None


def kernel(**inputs):
    n = 8
    x = np.ascontiguousarray(np.asarray(inputs["L0_attn_h"], dtype=np.float32))
    B, T, Dm = x.shape
    shards = x.reshape(n, (B // n) * T, Dm)
    if "nc" not in _NC_CACHE:
        _NC_CACHE["nc"] = _build()
    nc = _NC_CACHE["nc"]
    base = {k: np.ascontiguousarray(np.asarray(inputs[k], dtype=np.float32))
            for k in ("L0_attn_W0", "L0_attn_W1", "L0_ffn_Wr",
                      "L0_ffn_W1", "L0_ffn_b1", "L0_ffn_W2", "L0_ffn_b2")}
    base["L0_ffn_br"] = np.ascontiguousarray(
        np.asarray(inputs["L0_ffn_br"], dtype=np.float32).reshape(1, E))
    in_maps = [dict(base, L0_attn_h=np.ascontiguousarray(shards[i])) for i in range(n)]
    res = run_bass_kernel_spmd(nc, in_maps, core_ids=list(range(n)))
    global _LAST_RESULTS
    _LAST_RESULTS = res.results
    outs = [np.asarray(r["out"], dtype=np.float32) for r in res.results]
    return np.concatenate(outs, axis=0).reshape(B, T, Dm)


# version 04
# speedup vs baseline: 1.0077x; 1.0077x over previous
import numpy as np
import concourse.bass as bass
import concourse.mybir as mybir
from concourse.bass_utils import run_bass_kernel_spmd
from contextlib import ExitStack

F32 = mybir.dt.float32
BF16 = mybir.dt.bfloat16
AF = mybir.ActivationFunctionType
ALU = mybir.AluOpType
AX = mybir.AxisListType

D = 1024
KD = 8
H = 2048
NHT = 16
E = 8
TPC = 4096
NT = 32
NTH = 16
TC = 512
NTC = 4
TB = 2048
NCT = 8


class Sem:
    def __init__(self, stack, nc, name):
        self.h = stack.enter_context(nc.semaphore(name))
        self.n = 0

    def inc(self, instr, k=1):
        instr.then_inc(self.h, k)
        self.n += k
        return self.n


class PSem:
    def __init__(self, stack, nc, name, ways=2):
        self.s = [Sem(stack, nc, f"{name}_{i}") for i in range(ways)]

    def inc(self, instr, idx, k=16):
        i = idx % len(self.s)
        self.s[i].inc(instr, k)
        return (i, self.s[i].n)

    def wait(self, eng, val):
        i, n = val
        eng.wait_ge(self.s[i].h, n)


def _build():
    nc = bass.Bass("TRN2", target_bir_lowering=False)

    x0 = nc.dram_tensor("L0_attn_h", [TPC, D], F32, kind="ExternalInput").ap()
    w0a = nc.dram_tensor("L0_attn_W0", [D, 3 * D], F32, kind="ExternalInput").ap()
    w1a = nc.dram_tensor("L0_attn_W1", [D, 3 * D], F32, kind="ExternalInput").ap()
    wr = nc.dram_tensor("L0_ffn_Wr", [D, E], F32, kind="ExternalInput").ap()
    br = nc.dram_tensor("L0_ffn_br", [1, E], F32, kind="ExternalInput").ap()
    w1f = nc.dram_tensor("L0_ffn_W1", [E, D, H], F32, kind="ExternalInput").ap()
    b1f = nc.dram_tensor("L0_ffn_b1", [E, H], F32, kind="ExternalInput").ap()
    w2f = nc.dram_tensor("L0_ffn_W2", [E, H, D], F32, kind="ExternalInput").ap()
    b2f = nc.dram_tensor("L0_ffn_b2", [E, D], F32, kind="ExternalInput").ap()
    out = nc.dram_tensor("out", [TPC, D], F32, kind="ExternalOutput").ap()
    if _DEBUG_OUTS:
        x1s = nc.dram_tensor("x1s", [TPC, D], F32, kind="ExternalOutput").ap()
        x2s = nc.dram_tensor("x2s", [TPC, D], F32, kind="ExternalOutput").ap()
        ats = nc.dram_tensor("ats", [TPC, D], F32, kind="ExternalOutput").ap()
    else:
        x1s = nc.dram_tensor("x1s", [TPC, D], F32).ap()
        x2s = nc.dram_tensor("x2s", [TPC, D], F32).ap()
        ats = nc.dram_tensor("ats", [TPC, D], F32).ap()

    with ExitStack() as st:
        ent = st.enter_context
        ident_b = ent(nc.sbuf_tensor("ident_b", [128, 128], BF16))
        ident_f = ent(nc.sbuf_tensor("ident_f", [128, 128], F32))
        iot_p = ent(nc.sbuf_tensor("iot_p", [128, 1], F32))
        iot_f = ent(nc.sbuf_tensor("iot_f", [128, 128], F32))
        neghalf = ent(nc.sbuf_tensor("neghalf", [128, 1], F32))
        ones_b = ent(nc.sbuf_tensor("ones_b", [1, 128], BF16))
        brb = ent(nc.sbuf_tensor("brb", [1, E], BF16))
        wrb = ent(nc.sbuf_tensor("wrb", [128, KD * E], BF16))
        b1s = ent(nc.sbuf_tensor("b1s", [128, NHT * E], F32))
        b2b = ent(nc.sbuf_tensor("b2b", [E, D], BF16))
        b1raw = ent(nc.sbuf_tensor("b1raw", [E, H], F32))

        def S(name):
            return Sem(st, nc, name)

        def P(name):
            return PSem(st, nc, name)

        g0 = S("g0"); c0 = S("c0"); v0 = S("v0"); t0 = S("t0"); e0 = S("e0")
        with nc.Block() as block:
            @block.gpsimd
            def _(g):
                c0.inc(g.iota(iot_p[:, :], [[0, 1]], channel_multiplier=1,
                              allow_small_or_imprecise_dtypes=True))
                c0.inc(g.iota(iot_f[:, :], [[1, 128]], channel_multiplier=0,
                              allow_small_or_imprecise_dtypes=True))
                c0.inc(g.memset(neghalf[:, :], -0.5))
                c0.inc(g.memset(ones_b[:, :], 1.0))
                g0.inc(g.dma_start(out=brb[:, :], in_=br), 16)
                g0.inc(g.dma_start(out=wrb[:, :],
                                   in_=wr.rearrange("(k p) e -> p k e", p=128)), 16)
                g0.inc(g.dma_start(out=b2b[:, :], in_=b2f), 16)
                g0.inc(g.dma_start(out=b1raw[:, :], in_=b1f), 16)

            @block.vector
            def _(v):
                v.wait_ge(c0.h, 2)
                v0.inc(v.tensor_scalar(out=ident_b[:, :], in0=iot_f[:, :],
                                       scalar1=iot_p[:, :], scalar2=None,
                                       op0=ALU.is_equal))
                v0.inc(v.tensor_scalar(out=ident_f[:, :], in0=iot_f[:, :],
                                       scalar1=iot_p[:, :], scalar2=None,
                                       op0=ALU.is_equal))

        with nc.psum_tensor("p0t", [128, E], F32) as p0t, nc.Block() as block:
            for k in range(NHT):
                @block.tensor
                def _(t, k=k):
                    if k == 0:
                        t.wait_ge(v0.h, 2)
                        t.wait_ge(g0.h, 64)
                    else:
                        t.wait_ge(e0.h, k)
                    t0.inc(t.transpose(p0t[:, :], b1raw[0:E, k * 128:(k + 1) * 128],
                                       ident_f[0:E, 0:E]))

                @block.vector
                def _(v, k=k):
                    v.wait_ge(t0.h, k + 1)
                    e0.inc(v.tensor_copy(out=b1s[:, k * E:(k + 1) * E], in_=p0t[:, :]))

        def emit_ln_tile(block, t, xin, sqs, cols, lnc_p, lnT, ptrp, sems, rec,
                        pre_scalar_wait=None):
            qx, va, sL, tT, vE = sems["qx"], sems["va"], sems["sL"], sems["tT"], sems["vE"]
            p = t % 2
            c = p * 12
            tr = t % NTH
            QC = 0x5f3759df + 1

            @block.vector
            def _(v):
                qx.wait(v, rec[("qx", t)])
                pv = rec.get(("pre_vec", t))
                if pv:
                    v.wait_ge(va.h, pv)
                va.inc(v.tensor_reduce(out=cols[:, c:c + 1], in_=xin[:, :],
                                       axis=AX.X, op=ALU.add))
                va.inc(v.tensor_tensor(out=sqs[:, :], in0=xin[:, :], in1=xin[:, :],
                                       op=ALU.mult))
                v.wait_ge(va.h, va.n)
                va.inc(v.tensor_reduce(out=cols[:, c + 3:c + 4], in_=sqs[:, :],
                                       axis=AX.X, op=ALU.add))
                v.wait_ge(va.h, va.n)
                va.inc(v.tensor_scalar(out=cols[:, c + 1:c + 2], in0=cols[:, c:c + 1],
                                       scalar1=1.0 / D, scalar2=None, op0=ALU.mult))
                v.wait_ge(va.h, va.n)
                va.inc(v.tensor_tensor(out=cols[:, c + 4:c + 5], in0=cols[:, c + 1:c + 2],
                                       in1=cols[:, c + 1:c + 2], op=ALU.mult))
                va.inc(v.tensor_scalar(out=cols[:, c + 3:c + 4], in0=cols[:, c + 3:c + 4],
                                       scalar1=1.0 / D, scalar2=None, op0=ALU.mult))
                v.wait_ge(va.h, va.n)
                va.inc(v.tensor_tensor(out=cols[:, c + 5:c + 6], in0=cols[:, c + 3:c + 4],
                                       in1=cols[:, c + 4:c + 5], op=ALU.subtract))
                v.wait_ge(va.h, va.n)
                va.inc(v.tensor_scalar(out=cols[:, c + 6:c + 7], in0=cols[:, c + 5:c + 6],
                                       scalar1=1e-5, scalar2=None, op0=ALU.add))
            rec[("u", t)] = va.n

            @block.scalar
            def _(s):
                s.wait_ge(va.h, rec[("u", t)])
                sL.inc(s.activation(out=cols[:, c + 7:c + 8], in_=cols[:, c + 6:c + 7],
                                    func=AF.Sqrt, scale=1.0))
            rec[("sig", t)] = sL.n

            @block.vector
            def _(v):
                v.wait_ge(sL.h, rec[("sig", t)])
                va.inc(v.reciprocal(out=cols[:, c + 8:c + 9], in_=cols[:, c + 7:c + 8]))
                v.wait_ge(va.h, va.n)
                va.inc(v.scalar_tensor_tensor(out=cols[:, c + 11:c + 12],
                                              in0=cols[:, c + 1:c + 2], scalar=-1.0,
                                              in1=cols[:, c + 8:c + 9],
                                              op0=ALU.mult, op1=ALU.mult))
            rec[("nmr", t)] = va.n

            @block.scalar
            def _(s):
                s.wait_ge(va.h, rec[("nmr", t)])
                if pre_scalar_wait is not None:
                    pre_scalar_wait(s)
                u = rec.get(("lncuse", p))
                if u:
                    s.wait_ge(tT.h, u)
                sL.inc(s.activation(out=lnc_p[:, :], in_=xin[:, :], func=AF.Identity,
                                    bias=cols[:, c + 11:c + 12],
                                    scale=cols[:, c + 8:c + 9]))
            rec[("ln", t)] = sL.n

            @block.tensor
            def _(t_):
                t_.wait_ge(sL.h, rec[("ln", t)])
                u = rec.get(("ptruse", p))
                if u:
                    t_.wait_ge(vE.h, u)
                for j in range(KD):
                    mm = t_.transpose(ptrp[:, j * 128:(j + 1) * 128],
                                      lnc_p[:, j * 128:(j + 1) * 128], ident_b[:, :])
                tT.inc(mm)
            rec[("tr", t)] = tT.n
            rec[("lncuse", p)] = tT.n

            @block.vector
            def _(v):
                v.wait_ge(tT.h, rec[("tr", t)])
                w = rec.get(("lnT_war",))
                if w:
                    eng_sem, val = w
                    v.wait_ge(eng_sem.h, val)
                dst = lnT.ap().rearrange("p (k tt cc) -> p k tt cc",
                                         k=KD, tt=NTH)[:, :, tr, :]
                vE.inc(v.tensor_copy(out=dst,
                                     in_=ptrp.ap().rearrange("p (k cc) -> p k cc", k=KD)))
            rec[("lnTev", t)] = vE.n
            rec[("ptruse", p)] = vE.n

        def emit_ffn(ci):
            sfx = f"f{ci}"
            with ExitStack() as fs, nc.Block() as block:
                fent = fs.enter_context
                xa = [fent(nc.sbuf_tensor(f"xa{i}{sfx}", [128, D], F32)) for i in range(2)]
                xb = [fent(nc.sbuf_tensor(f"xb{i}{sfx}", [128, D], F32)) for i in range(2)] if ci else None
                sqs = fent(nc.sbuf_tensor(f"sqs{sfx}", [128, D], F32))
                lnc = [fent(nc.sbuf_tensor(f"lnc{i}{sfx}", [128, D], BF16)) for i in range(2)]
                cols = fent(nc.sbuf_tensor(f"cols{sfx}", [128, 24], F32))
                qin = fent(nc.sbuf_tensor(f"qin{sfx}", [128, NTH * E], F32))
                mskb = fent(nc.sbuf_tensor(f"mskb{sfx}", [128, NTH * E], F32))
                ssum = fent(nc.sbuf_tensor(f"ssum{sfx}", [128, NTH], F32))
                rcp = fent(nc.sbuf_tensor(f"rcp{sfx}", [128, NTH], F32))
                lnT = fent(nc.sbuf_tensor(f"lnT{sfx}", [128, KD * 2048], BF16))
                s_all = fent(nc.sbuf_tensor(f"sall{sfx}", [128, NT * E], F32))
                sallT = fent(nc.sbuf_tensor(f"sallT{sfx}", [E, NT * 128], BF16))
                w1s = fent(nc.sbuf_tensor(f"w1s{sfx}", [128, KD * H], BF16))
                w2s = fent(nc.sbuf_tensor(f"w2s{sfx}", [128, NHT * D], BF16))
                heb = fent(nc.sbuf_tensor(f"heb{sfx}", [128, NHT * TC], BF16))
                acc = fent(nc.sbuf_tensor(f"acc{sfx}", [128, NTH * D], BF16))
                xc = [fent(nc.sbuf_tensor(f"xc{i}{sfx}", [128, D], F32)) for i in range(2)]
                outb = [fent(nc.sbuf_tensor(f"outb{i}{sfx}", [128, D], F32)) for i in range(2)]
                ptr = [fent(nc.psum_tensor(f"ptr{i}{sfx}", [128, KD * 128], BF16)) for i in range(2)]
                pAB = [fent(nc.psum_tensor(f"pAB{i}{sfx}", [128, 512], F32)) for i in range(2)]
                pm1 = [fent(nc.psum_tensor(f"pm1{i}{sfx}", [128, TC], F32)) for i in range(2)]
                pm2 = [fent(nc.psum_tensor(f"pm2{i}{sfx}", [128, 512], F32)) for i in range(2)]

                qx = P("qx" + sfx)
                qc = P("qc" + sfx)
                qw2 = P("qw2" + sfx) if ci else None
                qoc = P("qoc" + sfx)
                gw1 = S("gw1" + sfx); gw2 = S("gw2" + sfx)
                gm = S("gm" + sfx)
                va = S("va" + sfx); vE = S("vE" + sfx); vB = S("vB" + sfx); vC = S("vC" + sfx)
                sL = S("sL" + sfx); sQ = S("sQ" + sfx); sE = S("sE" + sfx)
                tT = S("tT" + sfx); tL = S("tL" + sfx)
                tM1 = S("tM1" + sfx); tM2 = S("tM2" + sfx); tB2 = S("tB2" + sfx)
                sems = dict(qx=qx, va=va, sL=sL, tT=tT, vE=vE)
                rec = {}

                def passA(half):
                    for tr in range(NTH):
                        t = half * NTH + tr
                        p = t % 2

                        @block.sync
                        def _(sy):
                            u = rec.get(("ln", t - 2))
                            if u:
                                sy.wait_ge(sL.h, u)
                            u = rec.get(("xause", p))
                            if u:
                                qw2.wait(sy, u)
                            src = x0 if ci == 0 else x1s
                            qx.inc(sy.dma_start(out=xa[p][:, :],
                                                in_=src[t * 128:(t + 1) * 128, :]), t)
                            if ci:
                                rec[("qx", t)] = qx.inc(
                                    sy.dma_start(out=xb[p][:, :],
                                                 in_=ats[t * 128:(t + 1) * 128, :]), t)
                            else:
                                rec[("qx", t)] = (t % 2, qx.s[t % 2].n)

                        if ci:
                            @block.vector
                            def _(v):
                                qx.wait(v, rec[("qx", t)])
                                va.inc(v.tensor_tensor(out=xa[p][:, :], in0=xa[p][:, :],
                                                       in1=xb[p][:, :], op=ALU.add))
                            rec[("xv", t)] = va.n
                            rec[("pre_vec", t)] = va.n

                            @block.sync
                            def _(sy):
                                sy.wait_ge(va.h, rec[("xv", t)])
                                rec[("x2w", t)] = qw2.inc(
                                    sy.dma_start(out=x2s[t * 128:(t + 1) * 128, :],
                                                 in_=xa[p][:, :]), t)
                            rec[("xause", p)] = rec[("x2w", t)]

                        xin = xa[p]
                        pre = (lambda s, tt=t: s.wait_ge(va.h, rec[("xv", tt)])) if ci else None
                        emit_ln_tile(block, t, xin, sqs, cols, lnc[p], lnT, ptr[p],
                                     sems, rec, pre_scalar_wait=pre)

                        @block.tensor
                        def _(t_):
                            t_.wait_ge(vE.h, rec[("lnTev", t)])
                            u = rec.get(("pABuse_v", p))
                            if u:
                                t_.wait_ge(va.h, u)
                            u = rec.get(("pABuse_e", p))
                            if u:
                                t_.wait_ge(vE.h, u)
                            u = rec.get(("pABuse_c", p))
                            if u:
                                t_.wait_ge(vC.h, u)
                            for k in range(KD):
                                t_.matmul(pAB[p][:, 0:E],
                                          lnT[:, k * 2048 + tr * 128:k * 2048 + (tr + 1) * 128],
                                          wrb[:, k * E:(k + 1) * E],
                                          start=(k == 0), stop=False)
                            tL.inc(t_.matmul(pAB[p][:, 0:E], ones_b[:, :], brb[:, :],
                                             start=False, stop=True))
                        rec[("lg", t)] = tL.n

                        @block.vector
                        def _(v):
                            c = p * 12
                            v.wait_ge(tL.h, rec[("lg", t)])
                            va.inc(v.tensor_reduce(out=cols[:, c + 10:c + 11],
                                                   in_=pAB[p][:, 0:E], axis=AX.X,
                                                   op=ALU.max, negate=True))
                            v.wait_ge(va.h, va.n)
                            va.inc(v.tensor_scalar(out=qin[:, tr * E:(tr + 1) * E],
                                                   in0=pAB[p][:, 0:E],
                                                   scalar1=cols[:, c + 10:c + 11],
                                                   scalar2=None, op0=ALU.add))
                            va.inc(v.tensor_scalar(out=mskb[:, tr * E:(tr + 1) * E],
                                                   in0=pAB[p][:, 0:E],
                                                   scalar1=cols[:, c + 10:c + 11],
                                                   scalar2=0.0, op0=ALU.add, op1=ALU.is_ge))
                        rec[("mask", t)] = va.n
                        rec[("pABuse_v", p)] = va.n

                    @block.scalar
                    def _(s):
                        s.wait_ge(va.h, rec[("mask", half * NTH + NTH - 1)])
                        sQ.inc(s.activation(out=sqs[:, 0:NTH * E], in_=qin[:, :],
                                            func=AF.Exp, scale=1.0))
                    rec[("qexp", half)] = sQ.n

                    @block.vector
                    def _(v):
                        v.wait_ge(sQ.h, rec[("qexp", half)])
                        va.inc(v.tensor_reduce(
                            out=ssum[:, :],
                            in_=sqs.ap().rearrange("p (tt e) -> p tt e", e=E)[:, 0:NTH, :],
                            axis=AX.X, op=ALU.add))
                        v.wait_ge(va.h, va.n)
                        va.inc(v.reciprocal(out=rcp[:, :], in_=ssum[:, :]))
                    rec[("rcp", half)] = va.n

                    for tr in range(NTH):
                        t = half * NTH + tr
                        p = t % 2

                        @block.vector
                        def _(v):
                            if tr == 0:
                                v.wait_ge(va.h, rec[("rcp", half)])
                            va.inc(v.tensor_scalar(out=s_all[:, t * E:(t + 1) * E],
                                                   in0=mskb[:, tr * E:(tr + 1) * E],
                                                   scalar1=rcp[:, tr:tr + 1],
                                                   scalar2=0.5, op0=ALU.mult, op1=ALU.mult))
                        rec[("sall", t)] = va.n

                        @block.tensor
                        def _(t_):
                            t_.wait_ge(va.h, rec[("sall", t)])
                            u = rec.get(("pABuse_e", p))
                            if u:
                                t_.wait_ge(vE.h, u)
                            tT.inc(t_.transpose(pAB[p][0:E, 16:16 + 128],
                                                s_all[:, t * E:(t + 1) * E],
                                                ident_f[:, :]))
                        rec[("strT", t)] = tT.n

                        @block.vector
                        def _(v):
                            v.wait_ge(tT.h, rec[("strT", t)])
                            vE.inc(v.tensor_copy(out=sallT[0:E, t * 128:(t + 1) * 128],
                                                 in_=pAB[p][0:E, 16:16 + 128]))
                        rec[("strE", t)] = vE.n
                        rec[("pABuse_e", p)] = vE.n
                    rec[("sallend", half)] = va.n

                def passB(half):
                    @block.gpsimd
                    def _(g):
                        u = rec.get(("accuse",))
                        if u:
                            g.wait_ge(vC.h, u)
                            g.wait_ge(vB.h, rec[("accwrit",)])
                        gm.inc(g.memset(acc.ap(), 0.0))
                    rec[("ms", half)] = gm.n

                    for e in range(E):
                        eg = half * E + e

                        @block.gpsimd
                        def _(g):
                            u = rec.get(("mm1_last",))
                            if u:
                                g.wait_ge(tM1.h, u)
                            gw1.inc(g.dma_start(
                                out=w1s.ap(),
                                in_=w1f[e].rearrange("(k p) h -> p k h", p=128)), 16)
                        rec[("w1", eg)] = gw1.n

                        @block.gpsimd
                        def _(g):
                            u = rec.get(("mm2_last",))
                            if u:
                                g.wait_ge(tM2.h, u)
                            gw2.inc(g.dma_start(
                                out=w2s.ap(),
                                in_=w2f[e].rearrange("(k p) d -> p k d", p=128)), 16)
                        rec[("w2", eg)] = gw2.n

                        for tc in range(NTC):
                            for ht in range(NHT):
                                pp = ht % 2

                                @block.tensor
                                def _(t_):
                                    if ht == 0 and tc == 0:
                                        t_.wait_ge(gw1.h, rec[("w1", eg)])
                                        if e == 0:
                                            t_.wait_ge(vE.h, rec[("lnTev", half * NTH + NTH - 1)])
                                    u = rec.get(("pm1use", pp))
                                    if u:
                                        t_.wait_ge(sE.h, u)
                                    for k in range(KD):
                                        mm = t_.matmul(
                                            pm1[pp][:, :],
                                            w1s[:, k * H + ht * 128:k * H + (ht + 1) * 128],
                                            lnT[:, k * 2048 + tc * TC:k * 2048 + (tc + 1) * TC],
                                            start=(k == 0), stop=(k == KD - 1))
                                    tM1.inc(mm)
                                rec[("mm1", eg, tc, ht)] = tM1.n

                                @block.scalar
                                def _(s):
                                    s.wait_ge(tM1.h, rec[("mm1", eg, tc, ht)])
                                    u = rec.get(("hebuse",))
                                    if u:
                                        s.wait_ge(tM2.h, u)
                                    sE.inc(s.activation(
                                        out=heb[:, ht * TC:(ht + 1) * TC],
                                        in_=pm1[pp][:, :], func=AF.Relu,
                                        bias=b1s[:, ht * E + e:ht * E + e + 1],
                                        scale=1.0))
                                rec[("ev1", eg, tc, ht)] = sE.n
                                rec[("pm1use", pp)] = sE.n

                            for m in range(4):
                                gtile = tc * 4 + m
                                tglob = half * NTH + gtile
                                for dch in range(2):
                                    gi = (m * 2 + dch) % 2

                                    @block.tensor
                                    def _(t_):
                                        if m == 0 and dch == 0:
                                            t_.wait_ge(sE.h, rec[("ev1", eg, tc, NHT - 1)])
                                            if tc == 0:
                                                t_.wait_ge(gw2.h, rec[("w2", eg)])
                                        u = rec.get(("pm2use", gi))
                                        if u:
                                            t_.wait_ge(vB.h, u)
                                        for kh in range(NHT):
                                            mm = t_.matmul(
                                                pm2[gi][:, :],
                                                heb[:, kh * TC + m * 128:kh * TC + (m + 1) * 128],
                                                w2s[:, kh * D + dch * 512:kh * D + (dch + 1) * 512],
                                                start=(kh == 0), stop=(kh == NHT - 1))
                                        tM2.inc(mm)
                                    rec[("mm2", eg, tc, m, dch)] = tM2.n

                                    @block.vector
                                    def _(v):
                                        v.wait_ge(tM2.h, rec[("mm2", eg, tc, m, dch)])
                                        if e == 0:
                                            v.wait_ge(gm.h, rec[("ms", half)])
                                            v.wait_ge(va.h, rec[("sallend", half)])
                                        u = rec.get(("accslot", gtile, dch))
                                        if u:
                                            v.wait_ge(vB.h, u)
                                        sl = acc[:, gtile * D + dch * 512:
                                                 gtile * D + (dch + 1) * 512]
                                        vB.inc(v.scalar_tensor_tensor(
                                            out=sl, in0=pm2[gi][:, :],
                                            scalar=s_all[:, tglob * E + e:tglob * E + e + 1],
                                            in1=sl, op0=ALU.mult, op1=ALU.add))
                                    rec[("pm2use", gi)] = vB.n
                                    rec[("accslot", gtile, dch)] = vB.n
                            rec[("hebuse",)] = tM2.n
                        rec[("mm1_last",)] = tM1.n
                        rec[("mm2_last",)] = tM2.n
                    rec[("accwrit",)] = vB.n
                    if half == 0:
                        rec[("lnT_war",)] = (tM1, tM1.n)

                def passC(half):
                    src = x0 if ci == 0 else x2s
                    dst = x1s if ci == 0 else out
                    for tr in range(NTH):
                        t = half * NTH + tr
                        p = t % 2

                        @block.sync
                        def _(sy):
                            if ci:
                                qw2.wait(sy, rec[("x2w", t)])
                            u = rec.get(("xcuse", p))
                            if u:
                                sy.wait_ge(vC.h, u)
                            rec[("cqx", t)] = qc.inc(
                                sy.dma_start(out=xc[p][:, :],
                                             in_=src[t * 128:(t + 1) * 128, :]), t)

                        for dch in range(2):
                            @block.tensor
                            def _(t_):
                                if tr == 0 and dch == 0:
                                    t_.wait_ge(tM2.h, rec[("mm2_last",)])
                                    t_.wait_ge(vE.h, rec[("strE", half * NTH + NTH - 1)])
                                u = rec.get(("pABuse_c2", dch))
                                if u:
                                    t_.wait_ge(vC.h, u)
                                tB2.inc(t_.matmul(pAB[dch][:, 0:512],
                                                  sallT[0:E, t * 128:(t + 1) * 128],
                                                  b2b[0:E, dch * 512:(dch + 1) * 512],
                                                  start=True, stop=True))
                            rec[("b2", t, dch)] = tB2.n

                            @block.vector
                            def _(v):
                                v.wait_ge(tB2.h, rec[("b2", t, dch)])
                                u = rec.get(("outbuse", p))
                                if u:
                                    qoc.wait(v, u)
                                v.wait_ge(vB.h, rec[("accslot", tr, dch)])
                                sl = acc[:, tr * D + dch * 512:tr * D + (dch + 1) * 512]
                                vC.inc(v.scalar_tensor_tensor(
                                    out=outb[p][:, dch * 512:(dch + 1) * 512],
                                    in0=pAB[dch][:, 0:512], scalar=1.0, in1=sl,
                                    op0=ALU.mult, op1=ALU.add))
                            rec[("b2e", t, dch)] = vC.n
                            rec[("pABuse_c2", dch)] = vC.n
                            rec[("pABuse_c", dch)] = vC.n

                        @block.vector
                        def _(v):
                            qc.wait(v, rec[("cqx", t)])
                            v.wait_ge(vC.h, rec[("b2e", t, 1)])
                            vC.inc(v.tensor_tensor(out=outb[p][:, :], in0=outb[p][:, :],
                                                   in1=xc[p][:, :], op=ALU.add))
                        rec[("cadd", t)] = vC.n
                        rec[("xcuse", p)] = vC.n

                        @block.sync
                        def _(sy):
                            sy.wait_ge(vC.h, rec[("cadd", t)])
                            rec[("cw", t)] = qoc.inc(
                                sy.dma_start(out=dst[t * 128:(t + 1) * 128, :],
                                             in_=outb[p][:, :]), t)
                        rec[("outbuse", p)] = rec[("cw", t)]
                    rec[("accuse",)] = vC.n

                for half in (0, 1):
                    passA(half)
                    passB(half)
                    passC(half)

                @block.sync
                def _(sy):
                    for ps in qoc.s:
                        sy.wait_ge(ps.h, ps.n)
                    if ci:
                        for ps in qw2.s:
                            sy.wait_ge(ps.h, ps.n)

        def emit_attn():
            sfx = "at"
            with ExitStack() as fs, nc.Block() as block:
                fent = fs.enter_context
                xa = [fent(nc.sbuf_tensor(f"xa{i}{sfx}", [128, D], F32)) for i in range(2)]
                sqa = fent(nc.sbuf_tensor(f"sqa{sfx}", [128, D], F32))
                lnc = [fent(nc.sbuf_tensor(f"lnc{i}{sfx}", [128, D], BF16)) for i in range(2)]
                cols = fent(nc.sbuf_tensor(f"cols{sfx}", [128, 24], F32))
                lnTa = fent(nc.sbuf_tensor(f"lnTa{sfx}", [128, KD * TB], BF16))
                h1T = fent(nc.sbuf_tensor(f"h1T{sfx}", [128, KD * TB], BF16))
                wsl = [[fent(nc.sbuf_tensor(f"wsl{i}{j}{sfx}", [128, KD * 128], BF16))
                        for j in range(3)] for i in range(2)]
                cbu = [fent(nc.sbuf_tensor(f"cbu{i}{sfx}", [128, TB], F32)) for i in range(2)]
                vbu = [fent(nc.sbuf_tensor(f"vbu{i}{sfx}", [128, TB], F32)) for i in range(2)]
                zsb = [fent(nc.sbuf_tensor(f"zsb{i}{sfx}", [128, 512], F32)) for i in range(2)]
                obu = fent(nc.sbuf_tensor(f"obu{sfx}", [128, TB], F32))
                gpb = [fent(nc.sbuf_tensor(f"gpb{i}{sfx}", [128, TB], BF16)) for i in range(2)]
                hmn = [fent(nc.sbuf_tensor(f"hmn{i}{sfx}", [128, 512], F32)) for i in range(2)]
                gtb = [fent(nc.sbuf_tensor(f"gtb{i}{sfx}", [128, 512], F32)) for i in range(2)]
                dsc = fent(nc.sbuf_tensor(f"dsc{sfx}", [128, TB], F32))
                t2s = fent(nc.sbuf_tensor(f"t2s{sfx}", [128, TB], F32))
                hnf = fent(nc.sbuf_tensor(f"hnf{sfx}", [128, TB], F32))
                stg = fent(nc.sbuf_tensor(f"stg{sfx}", [128, TB], F32))
                ptr = [fent(nc.psum_tensor(f"ptr{i}{sfx}", [128, KD * 128], BF16)) for i in range(2)]
                pz = [fent(nc.psum_tensor(f"pz{i}{sfx}", [128, 512], F32)) for i in range(4)]
                pt = [fent(nc.psum_tensor(f"pt{i}{sfx}", [128, 512], F32)) for i in range(2)]

                qx = P("qx" + sfx)
                qo = S("qo" + sfx)
                gw = P("gw" + sfx)
                va = S("va" + sfx); vE = S("vE" + sfx); vM = S("vM" + sfx)
                sL = S("sL" + sfx); sV = S("sV" + sfx)
                tT = S("tT" + sfx); tZ = S("tZ" + sfx)
                sems = dict(qx=qx, va=va, sL=sL, tT=tT, vE=vE)
                rec = {}

                for b in range(2):
                    for tr in range(NTH):
                        t = b * NTH + tr
                        p = t % 2

                        @block.sync
                        def _(sy):
                            u = rec.get(("ln", t - 2))
                            if u:
                                sy.wait_ge(sL.h, u)
                            rec[("qx", t)] = qx.inc(
                                sy.dma_start(out=xa[p][:, :],
                                             in_=x1s[t * 128:(t + 1) * 128, :]), t)

                        emit_ln_tile(block, t, xa[p], sqa, cols, lnc[p], lnTa, ptr[p],
                                     sems, rec)

                    for it in range(2):
                        wsrc = w0a if it == 0 else w1a
                        hprev = lnTa if it == 0 else h1T
                        for ct in range(NCT):
                            wp = ct % 2
                            cp = ct % 2

                            @block.gpsimd
                            def _(g):
                                u = rec.get(("wsluse", wp))
                                if u:
                                    g.wait_ge(tZ.h, u)
                                for j, base in enumerate((D, 0, 2 * D)):
                                    wv = gw.inc(g.dma_start(
                                        out=wsl[wp][j].ap(),
                                        in_=wsrc[:, base + ct * 128:base + (ct + 1) * 128]
                                        .rearrange("(k p) cc -> p k cc", p=128)), ct)
                                rec[("wsl", b, it, ct)] = wv

                            for ch in range(4):
                                sl = slice(ch * 512, (ch + 1) * 512)
                                chp = ch % 2
                                for j in range(3):
                                    pzi = (ch * 3 + j) % 4

                                    @block.tensor
                                    def _(t_):
                                        if j == 0 and ch == 0:
                                            gw.wait(t_, rec[("wsl", b, it, ct)])
                                            if it == 0 and ct == 0:
                                                t_.wait_ge(vE.h, rec[("lnTev", b * NTH + NTH - 1)])
                                            if it == 1 and ct == 0:
                                                t_.wait_ge(vM.h, rec[("mixall", b, 0)])
                                        u = rec.get(("pzuse", pzi))
                                        if u:
                                            eng, val = u
                                            if eng == "s":
                                                t_.wait_ge(sV.h, val)
                                            else:
                                                t_.wait_ge(va.h, val)
                                        for k in range(KD):
                                            mm = t_.matmul(
                                                pz[pzi][:, :],
                                                wsl[wp][j][:, k * 128:(k + 1) * 128],
                                                hprev[:, k * TB + ch * 512:k * TB + (ch + 1) * 512],
                                                start=(k == 0), stop=(k == KD - 1))
                                        tZ.inc(mm)
                                    rec[("z", b, it, ct, j, ch)] = tZ.n

                                    if j == 0:
                                        @block.scalar
                                        def _(s):
                                            s.wait_ge(tZ.h, rec[("z", b, it, ct, j, ch)])
                                            u = rec.get(("cbuuse", cp))
                                            if u:
                                                s.wait_ge(vM.h, u)
                                            u = rec.get(("zsbuse", chp))
                                            if u:
                                                s.wait_ge(va.h, u)
                                            sV.inc(s.activation(out=cbu[cp][:, sl],
                                                                in_=pz[pzi][:, :],
                                                                func=AF.Sigmoid, scale=-1.0))
                                            sV.inc(s.activation(out=zsb[chp][:, :],
                                                                in_=pz[pzi][:, :],
                                                                func=AF.Sigmoid, scale=1.0))
                                        rec[("pzuse", pzi)] = ("s", sV.n)
                                        rec[("gate", b, it, ct, ch)] = sV.n
                                    elif j == 1:
                                        @block.vector
                                        def _(v):
                                            v.wait_ge(tZ.h, rec[("z", b, it, ct, j, ch)])
                                            u = rec.get(("hmnuse", chp))
                                            if u:
                                                v.wait_ge(sV.h, u)
                                            va.inc(v.tensor_scalar(out=hmn[chp][:, :],
                                                                   in0=pz[pzi][:, :],
                                                                   scalar1=0.0, scalar2=None,
                                                                   op0=ALU.min))
                                        rec[("hmn", b, it, ct, ch)] = va.n

                                        @block.scalar
                                        def _(s):
                                            s.wait_ge(va.h, rec[("hmn", b, it, ct, ch)])
                                            u = rec.get(("gtbuse", chp))
                                            if u:
                                                s.wait_ge(va.h, u)
                                            sV.inc(s.activation(out=gtb[chp][:, :],
                                                                in_=hmn[chp][:, :],
                                                                func=AF.Sigmoid, scale=1.0))
                                        rec[("sg", b, it, ct, ch)] = sV.n
                                        rec[("hmnuse", chp)] = sV.n

                                        @block.vector
                                        def _(v):
                                            v.wait_ge(sV.h, rec[("sg", b, it, ct, ch)])
                                            va.inc(v.scalar_tensor_tensor(
                                                out=gtb[chp][:, :], in0=pz[pzi][:, :],
                                                scalar=0.0, in1=gtb[chp][:, :],
                                                op0=ALU.max, op1=ALU.add))
                                            v.wait_ge(sV.h, rec[("gate", b, it, ct, ch)])
                                            v.wait_ge(va.h, va.n)
                                            va.inc(v.tensor_tensor(out=vbu[cp][:, sl],
                                                                   in0=gtb[chp][:, :],
                                                                   in1=zsb[chp][:, :],
                                                                   op=ALU.mult))
                                        rec[("pzuse", pzi)] = ("v", va.n)
                                        rec[("vv", b, it, ct, ch)] = va.n
                                        rec[("gtbuse", chp)] = va.n
                                        rec[("zsbuse", chp)] = va.n
                                    else:
                                        @block.scalar
                                        def _(s):
                                            s.wait_ge(tZ.h, rec[("z", b, it, ct, j, ch)])
                                            u = rec.get(("gpbuse", cp))
                                            if u:
                                                s.wait_ge(vM.h, u)
                                            sV.inc(s.activation(out=gpb[cp][:, sl],
                                                                in_=pz[pzi][:, :],
                                                                func=AF.Sigmoid, scale=1.0))
                                        rec[("pzuse", pzi)] = ("s", sV.n)
                                        rec[("gp", b, it, ct, ch)] = sV.n
                            rec[("wsluse", wp)] = tZ.n

                            @block.vector
                            def _(v):
                                v.wait_ge(sV.h, rec[("gate", b, it, ct, 3)])
                                v.wait_ge(va.h, rec[("vv", b, it, ct, 3)])
                                u = rec.get(("obuuse",))
                                if u:
                                    v.wait_ge(va.h, u)
                                vM.inc(v.tensor_tensor_scan(
                                    out=obu[:, :], data0=cbu[cp][:, :],
                                    data1=vbu[cp][:, :], initial=0.0,
                                    op0=ALU.mult, op1=ALU.add))
                            rec[("scan", b, it, ct)] = vM.n
                            rec[("cbuuse", cp)] = vM.n

                            hp = hprev[:, ct * TB:(ct + 1) * TB]

                            @block.vector
                            def _(v):
                                v.wait_ge(sV.h, rec[("gp", b, it, ct, 3)])
                                v.wait_ge(vM.h, rec[("scan", b, it, ct)])
                                if it == 1:
                                    u = rec.get(("hnfuse",))
                                    if u:
                                        v.wait_ge(tT.h, u)
                                    v.wait_ge(vM.h, rec[("mix", b, 0, ct)])
                                else:
                                    v.wait_ge(vE.h, rec[("lnTev", b * NTH + NTH - 1)])
                                    if b == 1:
                                        v.wait_ge(tZ.h, rec[("h1Tuse",)])
                                u = rec.get(("mixprev",))
                                if u:
                                    v.wait_ge(vM.h, u)
                                va.inc(v.tensor_tensor(out=dsc[:, :], in0=obu[:, :],
                                                       in1=hp, op=ALU.subtract))
                                rec[("obuuse",)] = va.n
                                v.wait_ge(va.h, va.n)
                                va.inc(v.tensor_tensor(out=t2s[:, :], in0=dsc[:, :],
                                                       in1=gpb[cp][:, :], op=ALU.mult))
                                v.wait_ge(va.h, va.n)
                                dst_h = h1T[:, ct * TB:(ct + 1) * TB] if it == 0 else hnf[:, :]
                                vM.inc(v.tensor_tensor(out=dst_h, in0=hp, in1=t2s[:, :],
                                                       op=ALU.add))
                            rec[("mix", b, it, ct)] = vM.n
                            rec[("mixprev",)] = vM.n
                            rec[("gpbuse", cp)] = vM.n
                            if ct == NCT - 1:
                                rec[("mixall", b, it)] = vM.n

                            if it == 1:
                                for grp in range(4):
                                    pti = grp % 2

                                    @block.tensor
                                    def _(t_):
                                        t_.wait_ge(vM.h, rec[("mix", b, it, ct)])
                                        u = rec.get(("ptuse", pti))
                                        if u:
                                            t_.wait_ge(vE.h, u)
                                        for q in range(4):
                                            mm = t_.transpose(
                                                pt[pti][:, q * 128:(q + 1) * 128],
                                                hnf[:, (grp * 4 + q) * 128:
                                                    (grp * 4 + q + 1) * 128],
                                                ident_f[:, :])
                                        tT.inc(mm)
                                    rec[("htr", b, ct, grp)] = tT.n

                                    @block.vector
                                    def _(v):
                                        v.wait_ge(tT.h, rec[("htr", b, ct, grp)])
                                        u = rec.get(("stguse",))
                                        if u:
                                            v.wait_ge(qo.h, u)
                                        vE.inc(v.tensor_copy(
                                            out=stg[:, grp * 512:(grp + 1) * 512],
                                            in_=pt[pti][:, :]))
                                    rec[("ptuse", pti)] = vE.n
                                    rec[("stge", b, ct, grp)] = vE.n
                                rec[("hnfuse",)] = tT.n

                                @block.sync
                                def _(sy):
                                    sy.wait_ge(vE.h, rec[("stge", b, ct, 3)])
                                    qo.inc(sy.dma_start(
                                        out=ats[b * TB:(b + 1) * TB,
                                                ct * 128:(ct + 1) * 128]
                                        .rearrange("(tt p) cc -> p tt cc", p=128),
                                        in_=stg.ap().rearrange("p (tt cc) -> p tt cc",
                                                               cc=128)), 16)
                                rec[("stgw", b, ct)] = qo.n
                                rec[("stguse",)] = qo.n

                    rec[("lnT_war",)] = (tZ, tZ.n)
                    rec[("h1Tuse",)] = tZ.n

                @block.sync
                def _(sy):
                    sy.wait_ge(qo.h, qo.n)

        emit_ffn(0)
        emit_attn()
        emit_ffn(1)

    return nc


_NC_CACHE = {}
_DEBUG_OUTS = False
_LAST_RESULTS = None


def kernel(**inputs):
    n = 8
    x = np.ascontiguousarray(np.asarray(inputs["L0_attn_h"], dtype=np.float32))
    B, T, Dm = x.shape
    shards = x.reshape(n, (B // n) * T, Dm)
    if "nc" not in _NC_CACHE:
        _NC_CACHE["nc"] = _build()
    nc = _NC_CACHE["nc"]
    base = {k: np.ascontiguousarray(np.asarray(inputs[k], dtype=np.float32))
            for k in ("L0_attn_W0", "L0_attn_W1", "L0_ffn_Wr",
                      "L0_ffn_W1", "L0_ffn_b1", "L0_ffn_W2", "L0_ffn_b2")}
    base["L0_ffn_br"] = np.ascontiguousarray(
        np.asarray(inputs["L0_ffn_br"], dtype=np.float32).reshape(1, E))
    in_maps = [dict(base, L0_attn_h=np.ascontiguousarray(shards[i])) for i in range(n)]
    res = run_bass_kernel_spmd(nc, in_maps, core_ids=list(range(n)))
    global _LAST_RESULTS
    _LAST_RESULTS = res.results
    outs = [np.asarray(r["out"], dtype=np.float32) for r in res.results]
    return np.concatenate(outs, axis=0).reshape(B, T, Dm)
